# version 11 (final)
# baseline (speedup 1.0000x reference)
import numpy as np
import ml_dtypes
from contextlib import ExitStack

import concourse.bass as bass
import concourse.mybir as mybir
from concourse.bass_utils import run_bass_kernel_spmd

F32 = mybir.dt.float32
BF16 = mybir.dt.bfloat16
I32 = mybir.dt.int32
AF = mybir.ActivationFunctionType
ALU = mybir.AluOpType
AX = mybir.AxisListType
BF = ml_dtypes.bfloat16

NC = 8
T = 8192
D = 2048
TS = T // NC
DCH = D // 128
LN_EPS = 1e-5

N_EXP = 16
I_MOE = 1024
N_HEADS = 16
HD = 128
H_FF = 8192

_PROG = {}



def build_entry(variant):
    assert variant in ("entry0", "entry_plain", "entry_moe")
    has_parts = variant != "entry0"
    routing = variant in ("entry0", "entry_moe")
    w2prep = variant == "entry0"

    nc = bass.Bass("TRN2", target_bir_lowering=False, detect_race_conditions=False)
    x_in = nc.dram_tensor("x_in", [D, TS], F32, kind="ExternalInput").ap()
    if has_parts:
        parts = nc.dram_tensor("parts", [DCH, NC, 128, TS], BF16, kind="ExternalInput").ap()
    if routing:
        wr = nc.dram_tensor("wr", [D, N_EXP], F32, kind="ExternalInput").ap()
        rbias = nc.dram_tensor("rbias", [1, N_EXP], F32, kind="ExternalInput").ap()
    if w2prep:
        w2s = nc.dram_tensor("w2s", [H_FF, D // NC], F32, kind="ExternalInput").ap()
        h128 = nc.dram_tensor("h128", [128, 128], BF16, kind="ExternalInput").ap()
        w2p_o = nc.dram_tensor("w2p_o", [H_FF, D // NC], BF16, kind="ExternalOutput").ap()
    if has_parts:
        x_out = nc.dram_tensor("x_out", [D, TS], F32, kind="ExternalOutput").ap()
    ln_out = nc.dram_tensor("ln_out", [D, TS], BF16, kind="ExternalOutput").ap()
    if routing:
        c_out = nc.dram_tensor("c_out", [TS, N_EXP], F32, kind="ExternalOutput").ap()

    ctx = ExitStack()
    sb = lambda *a: ctx.enter_context(nc.sbuf_tensor(*a))
    pst = lambda *a: ctx.enter_context(nc.psum_tensor(*a))
    S = lambda n: ctx.enter_context(nc.semaphore(n))

    TB = min(512, TS)
    NTB = TS // TB
    NTT = TS // 128

    xs = sb("xs", [128, DCH, TS], F32)
    lns = sb("lns", [128, DCH, TS], BF16)
    xsq = sb("xsq", [128, TB], F32)
    t128 = sb("t128", [128, TB], F32)
    stat = sb("stat", [1, 6, TB], F32)
    epsr = sb("epsr", [1, 1], F32)
    ones128 = sb("ones128", [128, 1], F32)
    ones1 = sb("ones1", [1, 128], F32)
    if has_parts:
        pbuf = sb("pbuf", [128, 2, NC, TS], BF16)
    if routing:
        wrs = sb("wrs", [128, DCH, N_EXP], BF16)
        rbs = sb("rbs", [1, N_EXP], F32)
        rt1 = sb("rt1", [128, N_EXP], F32)
        rt2 = sb("rt2", [128, N_EXP], F32)
        rt3 = sb("rt3", [128, N_EXP], F32)
        rtp = sb("rtp", [128, N_EXP], F32)
        rtm1 = sb("rtm1", [128, N_EXP], F32)
        rtm2 = sb("rtm2", [128, N_EXP], F32)
        rmaxd = sb("rmaxd", [128, 1], F32)
        rsumd = sb("rsumd", [128, 1], F32)
        rinvd = sb("rinvd", [128, 1], F32)
        v1d = sb("v1d", [128, 1], F32)
        v2d = sb("v2d", [128, 1], F32)
        dend = sb("dend", [128, 1], F32)
        dend2 = sb("dend2", [128, 1], F32)
        rdeni = sb("rdeni", [128, 1], F32)
        epscol = sb("epscol", [128, 1], F32)
        junkA = sb("junkA", [128, N_EXP], F32)
        junkB = sb("junkB", [128, N_EXP], F32)
        c_all = sb("c_all", [128, NTT, N_EXP], F32)
    if w2prep:
        w2b = sb("w2b", [128, H_FF // 128, D // NC], BF16)
        w2f = sb("w2f", [128, H_FF // 128, D // NC // 2], F32)
        tmpw = sb("tmpw", [128, D // NC // 2], F32)
        h128s = sb("h128s", [128, 128], BF16)

    ps_s = pst("ps_s", [1, TB], F32)
    ps_q = pst("ps_q", [1, TB], F32)
    ps_bm = pst("ps_bm", [128, TB], F32)
    ps_br = pst("ps_br", [128, TB], F32)
    if routing:
        ps_r = pst("ps_r", [128, N_EXP], F32)
    if w2prep:
        ps_w0 = pst("ps_w0", [128, D // NC], F32)
        ps_w1 = pst("ps_w1", [128, D // NC], F32)

    ld = S("ld"); pld0 = S("pld0"); pld1 = S("pld1"); pld = (pld0, pld1); vfree = S("vfree"); vred = S("vred")
    sq = S("sq"); tcons = S("tcons"); smm = S("smm"); stv1 = S("stv1")
    stsq = S("stsq"); statv = S("statv"); bcast = S("bcast"); norm = S("norm")
    lgtm = S("lgtm"); rfree = S("rfree"); rv1 = S("rv1"); rv2 = S("rv2")
    rdone = S("rdone"); gld = S("gld"); w2ld = S("w2ld"); w2mm = S("w2mm"); w2ev = S("w2ev")
    w2bt = S("w2bt"); w2cast = S("w2cast"); st = S("st")

    block = ctx.enter_context(nc.Block())

    n_sync_ld = 0

    @block.sync
    def _(s):
        s.dma_start(out=xs[:], in_=x_in.rearrange("(c k) t -> k c t", k=128)).then_inc(ld, 16)
        if routing:
            s.dma_start(out=rbs[:], in_=rbias[:, :]).then_inc(ld, 16)
        if w2prep:
            s.dma_start(out=h128s[:], in_=h128[:, :]).then_inc(ld, 16)

    n_sync_ld = 16 * (1 + (1 if routing else 0) + (1 if w2prep else 0))
    n_gld = 16 if routing else 0

    @block.gpsimd
    def _(g):
        if routing:
            g.dma_start(out=wrs[:], in_=wr.rearrange("(c k) e -> k c e", k=128)).then_inc(gld, 16)
        if w2prep:
            g.dma_start(out=w2b[:], in_=w2s.rearrange("(c k) d -> k c d", k=128)).then_inc(w2ld, 16)
        if has_parts:
            for j in range(DCH):
                if j >= 2:
                    g.wait_ge(vred, j - 1)
                g.dma_start(out=pbuf[:, j % 2, :, :],
                            in_=parts[j].rearrange("n p t -> p n t")).then_inc(pld[j % 2], 16)
        n_st = 0
        if has_parts:
            g.wait_ge(vred, DCH)
            g.dma_start(out=x_out.rearrange("(c k) t -> k c t", k=128), in_=xs[:]).then_inc(st, 16)
            n_st += 16
        g.wait_ge(norm, NTB)
        g.dma_start(out=ln_out.rearrange("(c k) t -> k c t", k=128), in_=lns[:]).then_inc(st, 16)
        n_st += 16
        if routing:
            g.wait_ge(rdone, NTT)
            g.dma_start(out=c_out.rearrange("(n p) e -> p n e", p=128), in_=c_all[:]).then_inc(st, 16)
            n_st += 16
        if w2prep:
            g.wait_ge(w2cast, 2 * (H_FF // 128))
            g.dma_start(out=w2p_o.rearrange("(c k) d -> k c d", k=128), in_=w2b[:]).then_inc(st, 16)
            n_st += 16
        g.wait_ge(st, n_st)

    @block.vector
    def _(v):
        v.memset(ones128[:], 1.0)
        v.memset(ones1[:], 1.0)
        v.memset(epsr[:], LN_EPS)
        if has_parts:
            v.wait_ge(ld, n_sync_ld)
            for j in range(DCH):
                v.wait_ge(pld[j % 2], 16 * (j // 2 + 1))
                for p in range(NC):
                    _li = v.tensor_tensor(out=xs[:, j, :], in0=xs[:, j, :],
                                    in1=pbuf[:, j % 2, p, :], op=ALU.add)
                _li.then_inc(vred, 1)
        for tb in range(NTB):
            v.wait_ge(tcons, (tb + 1) * DCH)
            sl = slice(tb * TB, (tb + 1) * TB)
            v.tensor_scalar(out=stat[:, 0, :], in0=ps_s[:, :], scalar1=1.0 / D,
                            scalar2=None, op0=ALU.mult)
            v.tensor_scalar(out=stat[:, 1, :], in0=ps_q[:, :], scalar1=1.0 / D,
                            scalar2=None, op0=ALU.mult)
            v.tensor_tensor(out=stat[:, 2, :], in0=stat[:, 0, :], in1=stat[:, 0, :], op=ALU.mult)
            _li = v.tensor_tensor(out=stat[:, 1, :], in0=stat[:, 1, :], in1=stat[:, 2, :], op=ALU.subtract)
            _li.then_inc(stv1, 1)
            v.wait_ge(stsq, tb + 1)
            _li = v.reciprocal(out=stat[:, 4, :], in_=stat[:, 3, :])
            _li.then_inc(statv, 1)
            v.wait_ge(bcast, tb + 1)
            for j in range(DCH):
                v.tensor_tensor(out=t128[:], in0=xs[:, j, sl], in1=ps_bm[:, :], op=ALU.subtract)
                _li = v.tensor_tensor(out=lns[:, j, sl], in0=t128[:], in1=ps_br[:, :], op=ALU.mult)
            _li.then_inc(norm, 1)
        if routing:
            v.memset(epscol[:], 1e-9)
            v.memset(junkA[:], 1.0)
            v.memset(junkB[:], 2.0)
            sp = lambda: v.tensor_tensor(out=junkA[:], in0=junkA[:], in1=junkB[:], op=ALU.mult)
            bcv1 = bass.AP(v1d, 0, [[1, 128], [0, N_EXP]])
            bcv2 = bass.AP(v2d, 0, [[1, 128], [0, N_EXP]])
            for tt in range(NTT):
                v.wait_ge(lgtm, tt + 1)
                _li = v.tensor_copy(out=rt1[:], in_=ps_r[:, :])
                _li.then_inc(rfree, 1)
                sp()
                v.reduce_max(out=rmaxd[:], in_=rt1[:], axis=AX.X)
                sp()
                _li = v.tensor_scalar(out=rt2[:], in0=rt1[:], scalar1=rmaxd[:],
                                scalar2=None, op0=ALU.subtract)
                _li.then_inc(rv1, 1)
                v.wait_ge(rv2, tt + 1)
                v.reduce_sum(out=rsumd[:], in_=rt3[:], axis=AX.X)
                sp()
                v.reciprocal(out=rinvd[:], in_=rsumd[:])
                sp()
                v.tensor_scalar(out=rtp[:], in0=rt3[:], scalar1=rinvd[:],
                                scalar2=None, op0=ALU.mult)
                sp()
                v.reduce_max(out=v1d[:], in_=rtp[:], axis=AX.X)
                sp()
                v.tensor_tensor(out=rtm1[:], in0=rtp[:], in1=bcv1, op=ALU.is_equal)
                sp()
                v.tensor_scalar(out=rt1[:], in0=rtm1[:], scalar1=-1.0, scalar2=1.0,
                                op0=ALU.mult, op1=ALU.add)
                sp()
                v.tensor_tensor(out=rt2[:], in0=rtp[:], in1=rt1[:], op=ALU.mult)
                sp()
                v.reduce_max(out=v2d[:], in_=rt2[:], axis=AX.X)
                sp()
                v.tensor_tensor(out=rtm2[:], in0=rt2[:], in1=bcv2, op=ALU.is_equal)
                sp()
                v.tensor_tensor(out=rt1[:], in0=rtm1[:], in1=rtm2[:], op=ALU.add)
                sp()
                v.tensor_tensor(out=rt2[:], in0=rtp[:], in1=rt1[:], op=ALU.mult)
                v.tensor_tensor(out=dend[:], in0=v1d[:], in1=v2d[:], op=ALU.add)
                sp()
                v.tensor_tensor(out=dend2[:], in0=dend[:], in1=epscol[:], op=ALU.add)
                sp()
                v.reciprocal(out=rdeni[:], in_=dend2[:])
                sp()
                _li = v.tensor_scalar(out=c_all[:, tt, :], in0=rt2[:], scalar1=rdeni[:],
                                scalar2=0.5, op0=ALU.mult, op1=ALU.mult)
                _li.then_inc(rdone, 1)
        if w2prep:
            nch = H_FF // 128
            for half in range(2):
                v.wait_ge(w2ev, (half + 1) * nch)
                for s_ in range(6):
                    step = 1 << s_
                    for a in range(nch):
                        if a & step:
                            continue
                        b = a + step
                        v.tensor_tensor(out=tmpw[:], in0=w2f[:, a, :], in1=w2f[:, b, :], op=ALU.subtract)
                        v.tensor_tensor(out=w2f[:, a, :], in0=w2f[:, a, :], in1=w2f[:, b, :], op=ALU.add)
                        _li = v.tensor_copy(out=w2f[:, b, :], in_=tmpw[:])
                _li.then_inc(w2bt, 1)

    @block.scalar
    def _(sc):
        for tb in range(NTB):
            sl = slice(tb * TB, (tb + 1) * TB)
            for j in range(DCH):
                if has_parts:
                    sc.wait_ge(vred, j + 1 if tb == 0 else DCH)
                else:
                    sc.wait_ge(ld, n_sync_ld)
                k = tb * DCH + j
                if k > 0:
                    sc.wait_ge(tcons, k)
                _li = sc.activation(out=xsq[:], in_=xs[:, j, sl], func=AF.Square)
                _li.then_inc(sq, 1)
            sc.wait_ge(stv1, tb + 1)
            _li = sc.activation(out=stat[:, 3, :], in_=stat[:, 1, :], func=AF.Sqrt, bias=epsr[:, :])
            _li.then_inc(stsq, 1)
        if routing:
            for tt in range(NTT):
                sc.wait_ge(rv1, tt + 1)
                _li = sc.activation(out=rt3[:], in_=rt2[:], func=AF.Exp)
                _li.then_inc(rv2, 1)
        if w2prep:
            nch = H_FF // 128
            HW2 = D // NC // 2
            for half in range(2):
                for j in range(nch):
                    sc.wait_ge(w2mm, half * nch + j + 1)
                    _li = sc.copy(out=w2f[:, j, :], in_=[ps_w0, ps_w1][j % 2][:, :HW2])
                    _li.then_inc(w2ev, 1)
                sc.wait_ge(w2bt, half + 1)
                for j in range(nch):
                    _li = sc.copy(out=w2b[:, j, half * HW2:(half + 1) * HW2], in_=w2f[:, j, :])
                    _li.then_inc(w2cast, 1)

    @block.tensor
    def _(t):
        for tb in range(NTB):
            sl = slice(tb * TB, (tb + 1) * TB)
            if tb > 0:
                t.wait_ge(statv, tb)
                t.wait_ge(norm, tb)
            for j in range(DCH):
                t.wait_ge(sq, tb * DCH + j + 1)
                t.matmul(ps_s[:, :], ones128[:], xs[:, j, sl],
                         start=(j == 0), stop=(j == DCH - 1))
                mm = t.matmul(ps_q[:, :], ones128[:], xsq[:],
                              start=(j == 0), stop=(j == DCH - 1))
                mm.then_inc(tcons, 1)
            t.wait_ge(statv, tb + 1)
            t.matmul(ps_bm[:, :], ones1[:], stat[:, 0, :], start=True, stop=True)
            t.matmul(ps_br[:, :], ones1[:], stat[:, 4, :], start=True, stop=True).then_inc(bcast, 1)
        if routing:
            rbsb = None
            t.wait_ge(norm, NTB)
            t.wait_ge(ld, n_sync_ld)
            if n_gld:
                t.wait_ge(gld, n_gld)
            for tt in range(NTT):
                if tt > 0:
                    t.wait_ge(rfree, tt)
                tsl = slice(tt * 128, (tt + 1) * 128)
                for j in range(DCH):
                    t.matmul(ps_r[:, :], lns[:, j, tsl], wrs[:, j, :],
                             start=(j == 0), stop=False)
                t.matmul(ps_r[:, :], ones1[:], rbs[:, :], start=False, stop=True).then_inc(lgtm, 1)
        if w2prep:
            nch = H_FF // 128
            HW2 = D // NC // 2
            t.wait_ge(w2ld, 16)
            t.wait_ge(ld, n_sync_ld)
            for half in range(2):
                for j in range(nch):
                    k = half * nch + j
                    if k >= 2:
                        t.wait_ge(w2ev, k - 1)
                    t.matmul([ps_w0, ps_w1][k % 2][:, :HW2], h128s[:],
                             w2b[:, j, half * HW2:(half + 1) * HW2],
                             start=True, stop=True).then_inc(w2mm, 1)

    ctx.close()
    return nc



def build_moe():
    nc = bass.Bass("TRN2", target_bir_lowering=False, detect_race_conditions=False)
    ln = nc.dram_tensor("ln", [D, T], BF16, kind="ExternalInput").ap()
    wg = nc.dram_tensor("wg", [D, 2, I_MOE], F32, kind="ExternalInput").ap()
    wu = nc.dram_tensor("wu", [D, 2, I_MOE], F32, kind="ExternalInput").ap()
    wd = nc.dram_tensor("wd", [2, I_MOE, D], F32, kind="ExternalInput").ap()
    swg = nc.dram_tensor("swg", [D, 128], F32, kind="ExternalInput").ap()
    swu = nc.dram_tensor("swu", [D, 128], F32, kind="ExternalInput").ap()
    swd = nc.dram_tensor("swd", [128, D], F32, kind="ExternalInput").ap()
    crow = nc.dram_tensor("crow", [1, 2, T], F32, kind="ExternalInput").ap()
    part = nc.dram_tensor("part", [D, T], BF16, kind="ExternalOutput").ap()
    wuscr = nc.dram_tensor("wuscr", [D, 2, I_MOE], BF16, kind="Internal")
    wdscr = nc.dram_tensor("wdscr", [2, I_MOE, D], BF16, kind="Internal")

    ctx = ExitStack()
    sb = lambda *a: ctx.enter_context(nc.sbuf_tensor(*a))
    pst = lambda *a: ctx.enter_context(nc.psum_tensor(*a))
    S = lambda n: ctx.enter_context(nc.semaphore(n))

    TBm = 512
    NTBm = T // TBm
    NIC = I_MOE // 128
    KSL = 2 * NIC + 1

    wg_sb = sb("wg_sb", [128, DCH, 2, I_MOE], BF16)
    wu_sb = sb("wu_sb", [128, 2, DCH, 512], BF16)
    wd_sb = sb("wd_sb", [128, 2, NIC, 2, 256], BF16)
    swg_sb = sb("swg_sb", [128, DCH, 128], BF16)
    swu_sb = sb("swu_sb", [128, DCH, 128], BF16)
    swd_sb = sb("swd_sb", [128, D], BF16)
    ln_sb = sb("ln_sb", [128, 2, DCH, TBm], BF16)
    m_e = sb("m_e", [128, 2, NIC, TBm], BF16)
    m_sh = sb("m_sh", [128, TBm], BF16)
    gs = sb("gs", [128, 2, TBm], BF16)
    t128 = sb("t128", [128, TBm], F32)
    out_st = sb("out_st", [128, DCH, TBm], BF16)
    crow_sb = sb("crow_sb", [1, 2, 2, TBm], F32)
    ones1 = sb("ones1", [1, 128], F32)

    psG = [pst("psG0", [128, TBm], F32), pst("psG1", [128, TBm], F32)]
    psU = [pst("psU0", [128, TBm], F32), pst("psU1", [128, TBm], F32)]
    psC = [pst("psC0", [128, TBm], F32), pst("psC1", [128, TBm], F32)]
    psD = [pst("psD0", [128, TBm], F32), pst("psD1", [128, TBm], F32)]

    wld = S("wld"); lnl = S("lnl"); wul = S("wul"); wdl = S("wdl"); crl = S("crl")
    cbm = S("cbm"); gmm = S("gmm"); umm = S("umm"); sil = S("sil")
    mtt = S("mtt"); acons = S("acons"); bmm = S("bmm"); bev = S("bev")
    pstr = S("pstr")

    N_CAST = 16 * 6

    ucum = lambda tb, k: tb * KSL + k + 1

    block = ctx.enter_context(nc.Block())

    @block.gpsimd
    def _(g):
        g.dma_start(out=wg_sb[:], in_=wg.rearrange("(c k) e i -> k c e i", k=128)).then_inc(wld, 16)
        g.dma_start(out=swg_sb[:], in_=swg.rearrange("(c k) i -> k c i", k=128)).then_inc(wld, 16)
        g.dma_start(out=swu_sb[:], in_=swu.rearrange("(c k) i -> k c i", k=128)).then_inc(wld, 16)
        g.dma_start(out=swd_sb[:], in_=swd[:, :]).then_inc(wld, 16)
        g.dma_start(out=wuscr[:, :, :], in_=wu[:, :, :]).then_inc(wld, 16)
        g.dma_start(out=wdscr[:, :, :], in_=wd[:, :, :]).then_inc(wld, 16)
        g.wait_ge(wld, N_CAST)
        wdr = wdscr.ap().rearrange("e (ki k) d -> k ki e d", k=128)
        partr = part.rearrange("(c k) (n s) -> k c n s", k=128, s=TBm)
        for tb in range(NTBm):
            if tb >= 1:
                g.wait_ge(bev, DCH * tb)
                g.dma_start(out=partr[:, :, tb - 1, :], in_=out_st[:]).then_inc(pstr, 16)
            for o in range(8):
                nq = tb * 8 + o
                if nq >= 2:
                    tbp, op = divmod(nq - 2, 8)
                    g.wait_ge(bmm, tbp * DCH + op * 2 + 2)
                for e in range(2):
                    g.dma_start(out=wd_sb[:, nq % 2, :, e, :],
                                in_=wdr[:, :, e, o * 256:(o + 1) * 256]).then_inc(wdl, 16)
        g.wait_ge(bev, DCH * NTBm)
        g.dma_start(out=partr[:, :, NTBm - 1, :], in_=out_st[:]).then_inc(pstr, 16)
        g.wait_ge(pstr, 16 * NTBm)

    @block.sync
    def _(s):
        lnr = ln.rearrange("(c k) (n s) -> k c n s", k=128, s=TBm)
        wur = wuscr.ap().rearrange("(c k) e (h i) -> k c e h i", k=128, h=2)
        s.wait_ge(wld, N_CAST)
        for tb in range(NTBm):
            if tb >= 2:
                s.wait_ge(umm, (tb - 1) * KSL)
            s.dma_start(out=ln_sb[:, tb % 2, :, :], in_=lnr[:, :, tb, :]).then_inc(lnl, 16)
            if tb >= 2:
                s.wait_ge(cbm, tb - 1)
            s.dma_start(out=crow_sb[:, tb % 2, :, :],
                        in_=crow[:, :, tb * TBm:(tb + 1) * TBm]).then_inc(crl, 16)
            for eh in range(4):
                e, h = divmod(eh, 2)
                nn = tb * 4 + eh
                if nn >= 2:
                    tbp, ehp = divmod(nn - 2, 4)
                    ep, hp = divmod(ehp, 2)
                    s.wait_ge(umm, ucum(tbp, ep * NIC + hp * 4 + 3))
                s.dma_start(out=wu_sb[:, nn % 2, :, :],
                            in_=wur[:, :, e, h, :]).then_inc(wul, 16)

    @block.tensor
    def _(t):
        t.wait_ge(wld, N_CAST)
        for tb in range(NTBm):
            sl = slice(tb * TBm, (tb + 1) * TBm)
            if tb:
                t.wait_ge(mtt, tb * KSL)
            t.wait_ge(crl, 16 * (tb + 1))
            for e in range(2):
                _li = t.matmul(psC[e][:, :], ones1[:], crow_sb[:1, tb % 2, e, :], start=True, stop=True)
            _li.then_inc(cbm, 1)
            t.wait_ge(lnl, 16 * (tb + 1))
            for k in range(KSL):
                kg = tb * KSL + k
                if k < 16:
                    e, ic = divmod(k, NIC)
                    h, icw = divmod(ic, 4)
                    nn = tb * 4 + e * 2 + h
                    t.wait_ge(wul, 16 * (nn + 1))
                    wgt_g = lambda j: wg_sb[:, j, e, ic * 128:(ic + 1) * 128]
                    wgt_u = lambda j, nn=nn, icw=icw: wu_sb[:, nn % 2, j, icw * 128:(icw + 1) * 128]
                    bg, bu = psG[k % 2], psU[k % 2]
                else:
                    wgt_g = lambda j: swg_sb[:, j, :]
                    wgt_u = lambda j: swu_sb[:, j, :]
                    bg, bu = psG[0], psU[0]
                if kg >= 2:
                    t.wait_ge(sil, kg - 1)
                for j in range(DCH):
                    _li = t.matmul(bg[:, :], wgt_g(j), ln_sb[:, tb % 2, j, :],
                             start=(j == 0), stop=(j == DCH - 1))
                _li.then_inc(gmm, 1)
                if kg >= 2:
                    t.wait_ge(mtt, kg - 1)
                for j in range(DCH):
                    _li = t.matmul(bu[:, :], wgt_u(j), ln_sb[:, tb % 2, j, :],
                             start=(j == 0), stop=(j == DCH - 1))
                _li.then_inc(umm, 1)
            t.wait_ge(mtt, (tb + 1) * KSL)
            for dc in range(DCH):
                o = dc // 2
                nq = tb * 8 + o
                t.wait_ge(wdl, 32 * (nq + 1))
                bank = psD[dc % 2]
                gidx = tb * DCH + dc
                if gidx >= 2:
                    t.wait_ge(bev, gidx - 1)
                first = True
                for e in range(2):
                    for ic in range(NIC):
                        t.matmul(bank[:, :], wd_sb[:, nq % 2, ic, e, (dc % 2) * 128:(dc % 2 + 1) * 128],
                                 m_e[:, e, ic, :], start=first, stop=False)
                        first = False
                _li = t.matmul(bank[:, :], swd_sb[:, dc * 128:(dc + 1) * 128], m_sh[:],
                         start=False, stop=True)
                _li.then_inc(bmm, 1)

    @block.scalar
    def _(sc):
        for tb in range(NTBm):
            for k in range(KSL):
                kg = tb * KSL + k
                sc.wait_ge(gmm, kg + 1)
                if kg >= 2:
                    sc.wait_ge(mtt, kg - 1)
                bank = psG[k % 2] if k < 16 else psG[0]
                _li = sc.activation(out=gs[:, kg % 2, :], in_=bank[:, :], func=AF.Silu)
                _li.then_inc(sil, 1)
            for dc in range(DCH):
                gidx = tb * DCH + dc
                sc.wait_ge(bmm, gidx + 1)
                if dc == 0 and tb >= 1:
                    sc.wait_ge(pstr, 16 * tb)
                _li = sc.copy(out=out_st[:, dc, :], in_=psD[dc % 2][:, :])
                _li.then_inc(bev, 1)

    @block.vector
    def _(v):
        v.memset(ones1[:], 1.0)
        for tb in range(NTBm):
            v.wait_ge(cbm, tb + 1)
            if tb:
                v.wait_ge(bmm, tb * DCH)
            for k in range(KSL):
                kg = tb * KSL + k
                v.wait_ge(sil, kg + 1)
                v.wait_ge(umm, kg + 1)
                if k < 16:
                    e, ic = divmod(k, NIC)
                    v.tensor_tensor(out=t128[:], in0=gs[:, kg % 2, :],
                                    in1=psU[k % 2][:, :], op=ALU.mult)
                    _li = v.tensor_tensor(out=m_e[:, e, ic, :], in0=t128[:],
                                          in1=psC[e][:, :], op=ALU.mult)
                else:
                    _li = v.scalar_tensor_tensor(out=m_sh[:], in0=psU[0][:, :], scalar=0.5,
                                                 in1=gs[:, kg % 2, :], op0=ALU.mult, op1=ALU.mult)
                _li.then_inc(mtt, 1)

    ctx.close()
    return nc



QB0 = 256
NQB0 = T // QB0
ATTN0_DEBUG = False

def build_attn0():
    nc = bass.Bass("TRN2", target_bir_lowering=False, detect_race_conditions=False)
    ln = nc.dram_tensor("ln", [D, T], BF16, kind="ExternalInput").ap()
    wq = nc.dram_tensor("wq", [D, 4, 128], F32, kind="ExternalInput").ap()
    wk = nc.dram_tensor("wk", [D, 2, 128], F32, kind="ExternalInput").ap()
    wv = nc.dram_tensor("wv", [D, 128], F32, kind="ExternalInput").ap()
    wo = nc.dram_tensor("wo", [256, D], F32, kind="ExternalInput").ap()
    lq1 = nc.dram_tensor("lq1", [1, 2, 128], F32, kind="ExternalInput").ap()
    lq2 = nc.dram_tensor("lq2", [1, 2, 128], F32, kind="ExternalInput").ap()
    lk1 = nc.dram_tensor("lk1", [1, 2, 128], F32, kind="ExternalInput").ap()
    lk2 = nc.dram_tensor("lk2", [1, 2, 128], F32, kind="ExternalInput").ap()
    rmsw = nc.dram_tensor("rmsw", [128, 1], F32, kind="ExternalInput").ap()
    cs = nc.dram_tensor("cs", [64, 2, 2, T], F32, kind="ExternalInput").ap()
    maska = nc.dram_tensor("maska", [128, 2, QB0], BF16, kind="ExternalInput").ap()
    part = nc.dram_tensor("part", [D, T], BF16, kind="ExternalOutput").ap()
    qscr = nc.dram_tensor("qscr", [128, 4, T], BF16, kind="Internal")
    if ATTN0_DEBUG:
        dbg1 = nc.dram_tensor("dbg1", [128, 2, QB0], BF16, kind="ExternalOutput").ap()
        dbg2 = nc.dram_tensor("dbg2", [128, 2, QB0], BF16, kind="ExternalOutput").ap()
        dbg3 = nc.dram_tensor("dbg3", [128, 2, QB0], BF16, kind="ExternalOutput").ap()
        dbg4 = nc.dram_tensor("dbg4", [1, 8, QB0], F32, kind="ExternalOutput").ap()

    ctx = ExitStack()
    sb = lambda *a: ctx.enter_context(nc.sbuf_tensor(*a))
    pst = lambda *a: ctx.enter_context(nc.psum_tensor(*a))

    TBa = 512
    NTBa = T // TBa
    NKT = T // 128

    wq_sb = sb("wq_sb", [128, DCH, 4, 128], BF16)
    wk_sb = sb("wk_sb", [128, DCH, 2, 128], BF16)
    wv_sb = sb("wv_sb", [128, DCH, 128], BF16)
    wo_sb = sb("wo_sb", [128, 2, D], BF16)
    lam_in = sb("lam_in", [1, 4, 2, 128], BF16)
    lamv = sb("lamv", [1, 16], F32)
    rmsw_sb = sb("rmsw_sb", [128, 1], F32)
    shift = sb("shift", [128, 1], F32)
    cs_sb = sb("cs_sb", [64, 2, 2, 2, TBa], BF16)
    ln_sb = sb("ln_sb", [128, DCH, TBa], BF16)
    kT = sb("kT", [128, 2, T], BF16)
    v_sb = sb("v_sb", [128, NKT, 128], BF16)
    rtmp = sb("rtmp", [64, 2, TBa], BF16)
    imf = sb("imf", [64, TBa], BF16)
    rtf = sb("rtf", [128, QB0], F32)
    qrope = sb("qrope", [128, 4, TBa], BF16)
    qsl = sb("qsl", [128, 2, 4, QB0], BF16)
    w1 = sb("w1", [128, NKT, QB0], BF16)
    w2 = sb("w2", [128, NKT, QB0], BF16)
    mask_sb = sb("mask_sb", [128, 2, QB0], BF16)
    maska_sb = sb("maska_sb", [128, 2, QB0], BF16)
    rzb1 = sb("rzb1", [128, QB0], BF16)
    rzb2 = sb("rzb2", [128, QB0], BF16)
    drow = sb("drow", [1, 8, QB0], F32)
    tdf = sb("tdf", [128, 2, 2, QB0], F32)
    tA = sb("tA", [128, 2, QB0], BF16)
    tB = sb("tB", [128, 2, QB0], BF16)
    att_un = sb("att_un", [128, QB0], BF16)
    attq = sb("attq", [128, QB0], BF16)
    att_sb = sb("att_sb", [128, 2, QB0], BF16)
    out_st = sb("out_st", [128, 4, QB0], BF16)
    ones128 = sb("ones128", [128, 1], BF16)
    ones1 = sb("ones1", [1, 128], F32)

    pb = [pst("pb%d" % i, [128, 512], F32) for i in range(8)]

    SEMS = ("wld sld lnl csl0 csl1 pmm pev rop qst vev lamd qld smm sev msk zmm "
            "zrec zev zbc dtt dex avm aev asq ssm phi phs phsq phb attn womm wev pstr dbg").split()
    sem = {n: ctx.enter_context(nc.semaphore(n)) for n in SEMS}
    cnt = {n: 0 for n in SEMS}

    block = ctx.enter_context(nc.Block())
    SC = float(1.0 / np.sqrt(128.0))
    KTn = lambda qb: 2 * qb + 2

    def npairs(qb):
        return KTn(qb)

    smm_before_unit = {}
    sev_before_unit = {}
    dex_before_unit = {}
    acc_smm = 0
    for qb in range(NQB0):
        for h in range(2):
            u = qb * 2 + h
            smm_before_unit[u] = acc_smm
            acc_smm += npairs(qb)
    sev_before_unit = smm_before_unit
    acc_dex = 0
    for qb in range(NQB0):
        for h in range(2):
            u = qb * 2 + h
            dex_before_unit[u] = acc_dex
            acc_dex += KTn(qb) // 2

    @block.gpsimd
    def _(g):
        g.dma_start(out=wq_sb[:], in_=wq.rearrange("(c k) a d -> k c a d", k=128)).then_inc(sem["wld"], 16)
        g.dma_start(out=wk_sb[:], in_=wk.rearrange("(c k) a d -> k c a d", k=128)).then_inc(sem["wld"], 16)
        g.dma_start(out=wv_sb[:], in_=wv.rearrange("(c k) d -> k c d", k=128)).then_inc(sem["wld"], 16)
        g.dma_start(out=wo_sb[:], in_=wo.rearrange("(a k) d -> k a d", k=128)).then_inc(sem["wld"], 16)
        g.dma_start(out=lam_in[:, 0], in_=lq1[:, :, :]).then_inc(sem["wld"], 16)
        g.dma_start(out=lam_in[:, 1], in_=lq2[:, :, :]).then_inc(sem["wld"], 16)
        g.dma_start(out=lam_in[:, 2], in_=lk1[:, :, :]).then_inc(sem["wld"], 16)
        g.dma_start(out=lam_in[:, 3], in_=lk2[:, :, :]).then_inc(sem["wld"], 16)
        for tb0 in range(2):
            g.dma_start(out=cs_sb[:, tb0], in_=cs[:, :, :, tb0 * TBa:(tb0 + 1) * TBa]
                        ).then_inc(sem["csl%d" % (tb0 % 2)], 16)
        for tb in range(NTBa):
            g.wait_ge(sem["rop"], (tb + 1) * 6)
            g.dma_start(out=qscr.ap()[:, :, tb * TBa:(tb + 1) * TBa],
                        in_=qrope[:]).then_inc(sem["qst"], 16)
            nxt = tb + 2
            if nxt < NTBa:
                g.dma_start(out=cs_sb[:, nxt % 2], in_=cs[:, :, :, nxt * TBa:(nxt + 1) * TBa]
                            ).then_inc(sem["csl%d" % (nxt % 2)], 16)
        if ATTN0_DEBUG:
            g.wait_ge(sem["msk"], 2)
            g.dma_start(out=dbg1[:, :, :], in_=w1[:, 0:2, :]).then_inc(sem["dbg"], 16)
            g.dma_start(out=dbg2[:, :, :], in_=w2[:, 0:2, :]).then_inc(sem["dbg"], 16)
            g.wait_ge(sem["dex"], 1)
            g.dma_start(out=dbg3[:, :, :], in_=w1[:, 0:2, :]).then_inc(sem["dbg"], 16)
            g.wait_ge(sem["phi"], 1)
            g.dma_start(out=dbg4[:, :, 0:QB0], in_=drow[:, :, 0:QB0]).then_inc(sem["dbg"], 16)
        partr0 = part.rearrange("(c k) (n s) -> k c n s", k=128, s=QB0)
        for qb in range(NQB0):
            for dc in range(DCH):
                gi = qb * DCH + dc
                g.wait_ge(sem["wev"], gi + 1)
                g.dma_start(out=partr0[:, dc, qb, :],
                            in_=out_st[:, gi % 4, :]).then_inc(sem["pstr"], 16)
        g.wait_ge(sem["pstr"], 16 * NQB0 * DCH)

    @block.sync
    def _(s):
        s.dma_start(out=rmsw_sb[:], in_=rmsw[:, :]).then_inc(sem["sld"], 16)
        s.dma_start(out=maska_sb[:], in_=maska[:, :, :]).then_inc(sem["sld"], 16)
        lnr = ln.rearrange("(c k) (n s) -> k c n s", k=128, s=TBa)
        for tb in range(NTBa):
            if tb >= 1:
                s.wait_ge(sem["pmm"], tb * 10)
            s.dma_start(out=ln_sb[:], in_=lnr[:, :, tb, :]).then_inc(sem["lnl"], 16)
        for qb in range(NQB0):
            if qb >= 2:
                s.wait_ge(sem["attn"], 2 * (qb - 1))
            need_tb = min(NTBa, (qb * 2 + 2 + 3) // 4)
            s.wait_ge(sem["qst"], 16 * need_tb)
            s.dma_start(out=qsl[:, qb % 2], in_=qscr.ap()[:, :, qb * QB0:(qb + 1) * QB0]
                        ).then_inc(sem["qld"], 16)

    @block.tensor
    def _(t):
        t.wait_ge(sem["wld"], 16 * 8)
        for tb in range(NTBa):
            t.wait_ge(sem["lnl"], 16 * (tb + 1))
            for m in range(6):
                gi = tb * 6 + m
                if gi >= 2:
                    t.wait_ge(sem["rop"], gi - 1)
                bank = pb[m % 2]
                for j in range(DCH):
                    w = wq_sb[:, j, m, :] if m < 4 else wk_sb[:, j, m - 4, :]
                    _li = t.matmul(bank[:, :], w, ln_sb[:, j, :], start=(j == 0), stop=(j == DCH - 1))
                _li.then_inc(sem["pmm"], 1)
            for tt in range(4):
                kt = tb * 4 + tt
                bank = pb[2 + tt % 2]
                if kt >= 2:
                    t.wait_ge(sem["vev"], kt - 1)
                for j in range(DCH):
                    _li = t.matmul(bank[:, :128], ln_sb[:, j, tt * 128:(tt + 1) * 128], wv_sb[:, j, :],
                             start=(j == 0), stop=(j == DCH - 1))
                _li.then_inc(sem["pmm"], 1)
        nsc = 0
        ndex = 0
        nzbc = 0
        for qb in range(NQB0):
            K = KTn(qb)
            t.wait_ge(sem["qld"], 16 * (qb + 1))
            for h in range(2):
                u = qb * 2 + h
                for a in range(2):
                    for kp in range(0, K, 2):
                        ip = nsc
                        nsc += 1
                        if ip >= 2:
                            t.wait_ge(sem["sev"], ip - 1)
                        if ip == 0:
                            t.wait_ge(sem["vev"], NKT)
                            t.wait_ge(sem["rop"], 6 * NTBa)
                        if h == 0 and a == 0 and kp == 0 and qb >= 1:
                            t.wait_ge(sem["wev"], DCH * qb)
                        bank = pb[ip % 2]
                        for z in range(2):
                            _li = t.matmul(bank[:, z * QB0:(z + 1) * QB0],
                                     kT[:, a, (kp + z) * 128:(kp + z + 1) * 128],
                                     qsl[:, qb % 2, h * 2 + a, :],
                                     start=(z == 0), stop=(z == 1))
                        _li.then_inc(sem["smm"], 1)
                if ATTN0_DEBUG and u == 1:
                    t.wait_ge(sem["dbg"], 32)
                t.wait_ge(sem["msk"], 2 * (u + 1))
                if u:
                    t.wait_ge(sem["attn"], u)
                for a in range(2):
                    zb = pb[4 + a]
                    wsrc = w1 if a == 0 else w2
                    for kp in range(0, K, 2):
                        _li = t.matmul(zb[:1, :512], ones128[:],
                                 wsrc[:, kp:kp + 2, :].rearrange("k n q -> k (n q)"),
                                 start=(kp == 0), stop=(kp + 2 >= K))
                _li.then_inc(sem["zmm"], 1)
                t.wait_ge(sem["zrec"], u + 1)
                if u:
                    t.wait_ge(sem["dtt"], dex_before_unit[u])
                t.matmul(pb[2][:, :QB0], ones1[:], drow[:, 2, :QB0], start=True, stop=True)
                _li = t.matmul(pb[3][:, :QB0], ones1[:], drow[:, 3, :QB0], start=True, stop=True)
                _li.then_inc(sem["zbc"], 1)
                for kt in range(K):
                    gidx = dex_before_unit[u] + kt // 2 + 1
                    t.wait_ge(sem["dex"], gidx)
                    t.matmul(pb[7][:, :QB0], v_sb[:, kt, :], w1[:, kt, :],
                             start=(kt == 0), stop=(kt == K - 1))
                for kp in range(0, K, 2):
                    _li = t.matmul(pb[6][:1, :512], ones128[:],
                             w1[:, kp:kp + 2, :].rearrange("k n q -> k (n q)"),
                             start=(kp == 0), stop=(kp + 2 >= K))
                _li.then_inc(sem["avm"], 1)
                t.wait_ge(sem["asq"], u + 1)
                _li = t.matmul(pb[4][:1, :QB0], ones128[:], attq[:], start=True, stop=True)
                _li.then_inc(sem["ssm"], 1)
                t.wait_ge(sem["phi"], u + 1)
                _li = t.matmul(pb[5][:, :QB0], ones1[:], drow[:, 6, :QB0], start=True, stop=True)
                _li.then_inc(sem["phb"], 1)
            t.wait_ge(sem["attn"], 2 * (qb + 1))
            for dc in range(DCH):
                gi = qb * DCH + dc
                bank = pb[dc % 2]
                if gi >= 2:
                    t.wait_ge(sem["wev"], gi - 1)
                else:
                    t.wait_ge(sem["sev"], nsc)
                for hh in range(2):
                    _li = t.matmul(bank[:, :QB0], wo_sb[:, hh, dc * 128:(dc + 1) * 128],
                             att_sb[:, hh, :], start=(hh == 0), stop=(hh == 1))
                _li.then_inc(sem["womm"], 1)

    @block.scalar
    def _(sc):
        for tb in range(NTBa):
            for tt in range(4):
                sc.wait_ge(sem["pmm"], tb * 10 + 6 + tt + 1)
                _li = sc.copy(out=v_sb[:, tb * 4 + tt, :], in_=pb[2 + tt % 2][:, :128])
                _li.then_inc(sem["vev"], 1)
        sc.wait_ge(sem["lamd"], 1)
        _li = sc.activation(out=lamv[:1, 4:8], in_=lamv[:1, 0:4], func=AF.Exp)
        _li.then_inc(sem["lamd"], 1)
        nsc = 0
        for qb in range(NQB0):
            K = KTn(qb)
            for h in range(2):
                u = qb * 2 + h
                for a in range(2):
                    for kp in range(0, K, 2):
                        ip = nsc
                        nsc += 1
                        sc.wait_ge(sem["smm"], ip + 1)
                        wdst = (w1 if a == 0 else w2)[:, kp:kp + 2, :]
                        _li = sc.activation(out=wdst.rearrange("k n q -> k (n q)"),
                                      in_=pb[ip % 2][:, :512],
                                      func=AF.Exp, scale=SC, bias=shift[:, :])
                        _li.then_inc(sem["sev"], 1)
                sc.wait_ge(sem["zbc"], u + 1)
                sc.copy(out=rzb1[:], in_=pb[2][:, :QB0])
                _li = sc.copy(out=rzb2[:], in_=pb[3][:, :QB0])
                _li.then_inc(sem["zev"], 1)
                ng = K // 2
                for gidx in range(ng):
                    if ATTN0_DEBUG and u == 0 and gidx == 0:
                        sc.wait_ge(sem["dbg"], 32)
                    sc.wait_ge(sem["dtt"], dex_before_unit[u] + gidx + 1)
                    sc.activation(out=w1[:, gidx * 2:gidx * 2 + 2, :].rearrange("k n q -> k (n q)"),
                                  in_=tdf[:, gidx % 2, :, :].rearrange("k n q -> k (n q)"),
                                  func=AF.Exp).then_inc(sem["dex"], 1)
                sc.wait_ge(sem["avm"], u + 1)
                sc.copy(out=att_un[:], in_=pb[7][:, :QB0]).then_inc(sem["aev"], 1)
                sc.activation(out=attq[:], in_=att_un[:], func=AF.Square).then_inc(sem["asq"], 1)
                sc.wait_ge(sem["phs"], u + 1)
                _li = sc.activation(out=drow[:, 6, :QB0], in_=drow[:, 5, :QB0], func=AF.Sqrt)
                _li.then_inc(sem["phsq"], 1)
            for dc in range(DCH):
                gi = qb * DCH + dc
                sc.wait_ge(sem["womm"], gi + 1)
                if gi >= 4:
                    sc.wait_ge(sem["pstr"], 16 * (gi - 3))
                sc.copy(out=out_st[:, gi % 4, :], in_=pb[dc % 2][:, :QB0]).then_inc(sem["wev"], 1)

    @block.vector
    def _(v):
        v.memset(ones128[:], 1.0)
        v.memset(ones1[:], 1.0)
        v.memset(shift[:], -4.0)
        for tb in range(NTBa):
            v.wait_ge(sem["csl%d" % (tb % 2)], 16 * (tb // 2 + 1))
            v.wait_ge(sem["qst"], 16 * tb)
            for m in range(6):
                gi = tb * 6 + m
                a = (m % 2) if m < 4 else (m - 4)
                v.wait_ge(sem["pmm"], tb * 10 + m + 1)
                if m < 4:
                    dst = qrope[:, m, :]
                else:
                    dst = kT[:, m - 4, tb * TBa:(tb + 1) * TBa]
                cosr = cs_sb[:, tb % 2, a, 0, :]
                sinr = cs_sb[:, tb % 2, a, 1, :]
                v.tensor_copy(out=imf[:], in_=pb[m % 2][64:128, :])
                v.tensor_tensor(out=rtmp[:, 0, :], in0=pb[m % 2][0:64, :], in1=cosr, op=ALU.mult)
                v.tensor_tensor(out=rtmp[:, 1, :], in0=imf[:], in1=sinr, op=ALU.mult)
                v.tensor_tensor(out=dst[0:64, :], in0=rtmp[:, 0, :], in1=rtmp[:, 1, :], op=ALU.subtract)
                v.tensor_tensor(out=rtmp[:, 0, :], in0=pb[m % 2][0:64, :], in1=sinr, op=ALU.mult)
                v.tensor_tensor(out=rtmp[:, 1, :], in0=imf[:], in1=cosr, op=ALU.mult)
                _li = v.tensor_tensor(out=dst[64:128, :], in0=rtmp[:, 0, :], in1=rtmp[:, 1, :], op=ALU.add)
                _li.then_inc(sem["rop"], 1)
        v.wait_ge(sem["wld"], 16 * 8)
        v.wait_ge(sem["sld"], 16 * 2)
        v.tensor_scalar(out=mask_sb[:], in0=maska_sb[:], scalar1=0.0, scalar2=None,
                        op0=ALU.is_equal)
        v.tensor_tensor(out=lam_in[:, 0], in0=lam_in[:, 0], in1=lam_in[:, 2], op=ALU.mult)
        v.tensor_tensor(out=lam_in[:, 1], in0=lam_in[:, 1], in1=lam_in[:, 3], op=ALU.mult)
        _li = v.reduce_sum(out=lamv[:1, 0:4], in_=lam_in[:1, 0:2].rearrange("p a h d -> p (a h) d"),
                     axis=AX.X)
        _li.then_inc(sem["lamd"], 1)
        v.wait_ge(sem["lamd"], 2)
        v.tensor_tensor(out=lamv[:1, 8:10], in0=lamv[:1, 4:6], in1=lamv[:1, 6:8], op=ALU.subtract)
        _li = v.tensor_scalar(out=lamv[:1, 8:10], in0=lamv[:1, 8:10], scalar1=0.2, scalar2=None, op0=ALU.add)
        _li.then_inc(sem["lamd"], 1)
        for qb in range(NQB0):
            K = KTn(qb)
            for h in range(2):
                u = qb * 2 + h
                base = sev_before_unit[u]
                for a in range(2):
                    plast = base + (a + 1) * (K // 2)
                    v.wait_ge(sem["sev"], plast)
                    wdst = (w1 if a == 0 else w2)[:, K - 2:K, :]
                    _li = v.tensor_tensor(out=wdst, in0=wdst, in1=mask_sb[:, :, :], op=ALU.mult)
                    _li.then_inc(sem["msk"], 1)
                if ATTN0_DEBUG and u == 1:
                    v.wait_ge(sem["dbg"], 64)
                v.wait_ge(sem["zmm"], u + 1)
                v.tensor_copy(out=drow[:, 0:2, :].rearrange("p a q -> p (a q)"),
                              in_=pb[4][:1, 0:2 * QB0])
                v.tensor_tensor(out=drow[:, 0, :QB0], in0=drow[:, 0, :QB0],
                                in1=drow[:, 1, :QB0], op=ALU.add)
                v.reciprocal(out=drow[:, 2, :QB0], in_=drow[:, 0, :QB0])
                v.tensor_copy(out=drow[:, 0:2, :].rearrange("p a q -> p (a q)"),
                              in_=pb[5][:1, 0:2 * QB0])
                v.tensor_tensor(out=drow[:, 0, :QB0], in0=drow[:, 0, :QB0],
                                in1=drow[:, 1, :QB0], op=ALU.add)
                v.reciprocal(out=drow[:, 3, :QB0], in_=drow[:, 0, :QB0])
                _li = v.tensor_scalar(out=drow[:, 3, :QB0], in0=drow[:, 3, :QB0],
                                scalar1=lamv[:1, 8 + h:9 + h], scalar2=-1.0,
                                op0=ALU.mult, op1=ALU.mult)
                _li.then_inc(sem["zrec"], 1)
                v.wait_ge(sem["zev"], u + 1)
                ng = K // 2
                for gidx in range(ng):
                    if dex_before_unit[u] + gidx >= 2:
                        v.wait_ge(sem["dex"], dex_before_unit[u] + gidx - 1)
                    rz1bc = bass.AP(rzb1, 0, [[QB0, 128], [0, 2], [1, QB0]])
                    rz2bc = bass.AP(rzb2, 0, [[QB0, 128], [0, 2], [1, QB0]])
                    sl = slice(gidx * 2, gidx * 2 + 2)
                    v.tensor_tensor(out=tA[:], in0=w1[:, sl, :], in1=rz1bc, op=ALU.mult)
                    v.tensor_tensor(out=tB[:], in0=w2[:, sl, :], in1=rz2bc, op=ALU.mult)
                    _li = v.tensor_tensor(out=tdf[:, gidx % 2, :, :], in0=tA[:],
                                          in1=tB[:], op=ALU.add)
                    if gidx == ng - 1:
                        _li = v.tensor_tensor(out=tdf[:, gidx % 2, :, :],
                                              in0=tdf[:, gidx % 2, :, :],
                                              in1=maska_sb[:, :, :], op=ALU.add)
                    _li.then_inc(sem["dtt"], 1)
                v.wait_ge(sem["avm"], u + 1)
                v.tensor_copy(out=drow[:, 4:6, :].rearrange("p a q -> p (a q)"),
                              in_=pb[6][:1, 0:2 * QB0])
                v.tensor_tensor(out=drow[:, 4, :QB0], in0=drow[:, 4, :QB0],
                                in1=drow[:, 5, :QB0], op=ALU.add)
                v.reciprocal(out=drow[:, 4, :QB0], in_=drow[:, 4, :QB0])
                v.wait_ge(sem["ssm"], u + 1)
                v.tensor_scalar(out=drow[:, 5, :QB0], in0=pb[4][:1, :QB0],
                                scalar1=1.0 / 128.0, scalar2=None, op0=ALU.mult)
                v.tensor_tensor(out=drow[:, 7, :QB0], in0=drow[:, 4, :QB0],
                                in1=drow[:, 4, :QB0], op=ALU.mult)
                v.tensor_tensor(out=drow[:, 5, :QB0], in0=drow[:, 5, :QB0],
                                in1=drow[:, 7, :QB0], op=ALU.mult)
                _li = v.tensor_scalar(out=drow[:, 5, :QB0], in0=drow[:, 5, :QB0],
                                scalar1=1e-8, scalar2=None, op0=ALU.add)
                _li.then_inc(sem["phs"], 1)
                v.wait_ge(sem["phsq"], u + 1)
                v.reciprocal(out=drow[:, 7, :QB0], in_=drow[:, 6, :QB0])
                _li = v.tensor_tensor(out=drow[:, 6, :QB0], in0=drow[:, 7, :QB0],
                                in1=drow[:, 4, :QB0], op=ALU.mult)
                _li.then_inc(sem["phi"], 1)
                v.wait_ge(sem["phb"], u + 1)
                v.wait_ge(sem["aev"], u + 1)
                v.tensor_tensor(out=rtf[:], in0=att_un[:], in1=pb[5][:, :QB0], op=ALU.mult)
                _li = v.tensor_scalar(out=att_sb[:, h, :], in0=rtf[:], scalar1=rmsw_sb[:, :],
                                scalar2=0.8, op0=ALU.mult, op1=ALU.mult)
                _li.then_inc(sem["attn"], 1)

    ctx.close()
    return nc



def build_ffn1():
    nc = bass.Bass("TRN2", target_bir_lowering=False, detect_race_conditions=False)
    HS = H_FF // NC
    ln = nc.dram_tensor("ln", [D, T], BF16, kind="ExternalInput").ap()
    w1s = nc.dram_tensor("w1s", [D, HS], F32, kind="ExternalInput").ap()
    w3s = nc.dram_tensor("w3s", [D, HS], F32, kind="ExternalInput").ap()
    w2p = nc.dram_tensor("w2p", [HS, D], BF16, kind="ExternalInput").ap()
    part = nc.dram_tensor("part", [D, T], BF16, kind="ExternalOutput").ap()

    ctx = ExitStack()
    sb = lambda *a: ctx.enter_context(nc.sbuf_tensor(*a))
    pst = lambda *a: ctx.enter_context(nc.psum_tensor(*a))

    TBf = 512
    NTBf = T // TBf
    NIC = HS // 128

    w1_sb = sb("w1_sb", [128, DCH, HS], BF16)
    w3_sb = sb("w3_sb", [128, DCH, HS], BF16)
    w2_sb = sb("w2_sb", [128, NIC, D], BF16)
    ln_sb = sb("ln_sb", [128, 2, DCH, TBf], BF16)
    m_h = sb("m_h", [128, NIC, TBf], BF16)
    gs = sb("gs", [128, 2, TBf], BF16)
    out_st = sb("out_st", [128, DCH, TBf], BF16)

    psG = [pst("psG0", [128, TBf], F32), pst("psG1", [128, TBf], F32)]
    psU = [pst("psU0", [128, TBf], F32), pst("psU1", [128, TBf], F32)]
    psD = [pst("psD0", [128, TBf], F32), pst("psD1", [128, TBf], F32)]

    SEMS = "wld lnl gmm umm sil mtt bmm bev pstr".split()
    sem = {n: ctx.enter_context(nc.semaphore(n)) for n in SEMS}
    block = ctx.enter_context(nc.Block())
    OSC = float(0.5 / np.sqrt(float(H_FF)))

    @block.gpsimd
    def _(g):
        g.dma_start(out=w1_sb[:], in_=w1s.rearrange("(c k) i -> k c i", k=128)).then_inc(sem["wld"], 16)
        g.dma_start(out=w3_sb[:], in_=w3s.rearrange("(c k) i -> k c i", k=128)).then_inc(sem["wld"], 16)
        g.dma_start(out=w2_sb[:], in_=w2p.rearrange("(c k) d -> k c d", k=128)).then_inc(sem["wld"], 16)
        partr = part.rearrange("(c k) (n s) -> k c n s", k=128, s=TBf)
        for tb in range(NTBf):
            g.wait_ge(sem["bev"], DCH * (tb + 1))
            g.dma_start(out=partr[:, :, tb, :], in_=out_st[:]).then_inc(sem["pstr"], 16)
        g.wait_ge(sem["pstr"], 16 * NTBf)

    @block.sync
    def _(s):
        lnr = ln.rearrange("(c k) (n s) -> k c n s", k=128, s=TBf)
        for tb in range(NTBf):
            if tb >= 2:
                s.wait_ge(sem["umm"], (tb - 1) * NIC)
            s.dma_start(out=ln_sb[:, tb % 2], in_=lnr[:, :, tb, :]).then_inc(sem["lnl"], 16)

    @block.tensor
    def _(t):
        t.wait_ge(sem["wld"], 48)
        for tb in range(NTBf):
            t.wait_ge(sem["lnl"], 16 * (tb + 1))
            for ic in range(NIC):
                kg = tb * NIC + ic
                if kg >= 2:
                    t.wait_ge(sem["sil"], kg - 1)
                for j in range(DCH):
                    _li = t.matmul(psG[kg % 2][:, :], w1_sb[:, j, ic * 128:(ic + 1) * 128],
                             ln_sb[:, tb % 2, j, :], start=(j == 0), stop=(j == DCH - 1))
                _li.then_inc(sem["gmm"], 1)
                if kg >= 2:
                    t.wait_ge(sem["mtt"], kg - 1)
                for j in range(DCH):
                    _li = t.matmul(psU[kg % 2][:, :], w3_sb[:, j, ic * 128:(ic + 1) * 128],
                             ln_sb[:, tb % 2, j, :], start=(j == 0), stop=(j == DCH - 1))
                _li.then_inc(sem["umm"], 1)
            t.wait_ge(sem["mtt"], (tb + 1) * NIC)
            for dc in range(DCH):
                gi = tb * DCH + dc
                if gi >= 2:
                    t.wait_ge(sem["bev"], gi - 1)
                for ic in range(NIC):
                    _li = t.matmul(psD[dc % 2][:, :], w2_sb[:, ic, dc * 128:(dc + 1) * 128],
                             m_h[:, ic, :], start=(ic == 0), stop=(ic == NIC - 1))
                _li.then_inc(sem["bmm"], 1)

    @block.scalar
    def _(sc):
        for tb in range(NTBf):
            for ic in range(NIC):
                kg = tb * NIC + ic
                sc.wait_ge(sem["gmm"], kg + 1)
                if kg >= 2:
                    sc.wait_ge(sem["mtt"], kg - 1)
                _li = sc.activation(out=gs[:, kg % 2, :], in_=psG[kg % 2][:, :], func=AF.Silu)
                _li.then_inc(sem["sil"], 1)
            for dc in range(DCH):
                gi = tb * DCH + dc
                sc.wait_ge(sem["bmm"], gi + 1)
                if dc == 0 and tb >= 1:
                    sc.wait_ge(sem["pstr"], 16 * tb)
                _li = sc.activation(out=out_st[:, dc, :], in_=psD[dc % 2][:, :], func=AF.Copy, scale=OSC)
                _li.then_inc(sem["bev"], 1)

    @block.vector
    def _(v):
        for tb in range(NTBf):
            if tb:
                v.wait_ge(sem["bmm"], tb * DCH)
            for ic in range(NIC):
                kg = tb * NIC + ic
                v.wait_ge(sem["sil"], kg + 1)
                v.wait_ge(sem["umm"], kg + 1)
                _li = v.tensor_tensor(out=m_h[:, ic, :], in0=gs[:, kg % 2, :],
                                in1=psU[kg % 2][:, :], op=ALU.mult)
                _li.then_inc(sem["mtt"], 1)

    ctx.close()
    return nc



QB1 = 256
def build_attn1():
    nc = bass.Bass("TRN2", target_bir_lowering=False, detect_race_conditions=False)
    ln = nc.dram_tensor("ln", [D, T], BF16, kind="ExternalInput").ap()
    wqh = nc.dram_tensor("wqh", [D, 2, 128], F32, kind="ExternalInput").ap()
    wkh = nc.dram_tensor("wkh", [D, 2, 128], F32, kind="ExternalInput").ap()
    wvh = nc.dram_tensor("wvh", [D, 2, 128], F32, kind="ExternalInput").ap()
    woh = nc.dram_tensor("woh", [256, D], F32, kind="ExternalInput").ap()
    part = nc.dram_tensor("part", [D, T], BF16, kind="ExternalOutput").ap()

    ctx = ExitStack()
    sb = lambda *a: ctx.enter_context(nc.sbuf_tensor(*a))
    pst = lambda *a: ctx.enter_context(nc.psum_tensor(*a))

    TBa = 512
    NTBa = T // TBa
    NKT = T // 128
    NQB = T // QB1

    wq_sb = sb("wq_sb", [128, DCH, 2, 128], BF16)
    wk_sb = sb("wk_sb", [128, DCH, 2, 128], BF16)
    wv_sb = sb("wv_sb", [128, DCH, 2, 128], BF16)
    wo_sb = sb("wo_sb", [128, 2, D], BF16)
    shift = sb("shift", [128, 1], F32)
    ln_sb = sb("ln_sb", [128, DCH, TBa], BF16)
    qT = sb("qT", [128, 2, T], BF16)
    kT = sb("kT", [128, 2, T], BF16)
    v_sb = sb("v_sb", [128, NKT, 256], BF16)
    w1 = sb("w1", [128, NKT, QB1], BF16)
    drow = sb("drow", [1, 4, 512], F32)
    att_un = sb("att_un", [128, QB1], BF16)
    att_sb = sb("att_sb", [128, 2, QB1], BF16)
    out_st = sb("out_st", [128, DCH, QB1], BF16)
    ones128 = sb("ones128", [128, 1], BF16)
    ones1 = sb("ones1", [1, 128], F32)

    pb = [pst("pb%d" % i, [128, 512], F32) for i in range(8)]

    SEMS = "wld lnl pmm pev smm sev zmm zrec zbc avm aev attn womm wev pstr".split()
    sem = {n: ctx.enter_context(nc.semaphore(n)) for n in SEMS}
    block = ctx.enter_context(nc.Block())
    SC = float(1.0 / np.sqrt(128.0))

    @block.gpsimd
    def _(g):
        g.dma_start(out=wq_sb[:], in_=wqh.rearrange("(c k) h d -> k c h d", k=128)).then_inc(sem["wld"], 16)
        g.dma_start(out=wk_sb[:], in_=wkh.rearrange("(c k) h d -> k c h d", k=128)).then_inc(sem["wld"], 16)
        g.dma_start(out=wv_sb[:], in_=wvh.rearrange("(c k) h d -> k c h d", k=128)).then_inc(sem["wld"], 16)
        g.dma_start(out=wo_sb[:], in_=woh.rearrange("(a k) d -> k a d", k=128)).then_inc(sem["wld"], 16)
        for qb in range(NQB):
            g.wait_ge(sem["wev"], DCH * (qb + 1))
            g.dma_start(out=part.rearrange("(c k) (n s) -> k c n s", k=128, s=QB1)[:, :, qb, :],
                        in_=out_st[:]).then_inc(sem["pstr"], 16)
        g.wait_ge(sem["pstr"], 16 * NQB)

    @block.sync
    def _(s):
        lnr = ln.rearrange("(c k) (n s) -> k c n s", k=128, s=TBa)
        for tb in range(NTBa):
            if tb >= 1:
                s.wait_ge(sem["pmm"], tb * 8)
            s.dma_start(out=ln_sb[:], in_=lnr[:, :, tb, :]).then_inc(sem["lnl"], 16)

    @block.tensor
    def _(t):
        t.wait_ge(sem["wld"], 64)
        for tb in range(NTBa):
            t.wait_ge(sem["lnl"], 16 * (tb + 1))
            for m in range(4):
                gi = tb * 4 + m
                if gi >= 2:
                    t.wait_ge(sem["pev"], gi - 1)
                bank = pb[m % 2]
                w = wq_sb[:, :, m, :] if m < 2 else wk_sb[:, :, m - 2, :]
                for j in range(DCH):
                    _li = t.matmul(bank[:, :], w[:, j, :], ln_sb[:, j, :],
                             start=(j == 0), stop=(j == DCH - 1))
                _li.then_inc(sem["pmm"], 1)
            for tt in range(4):
                kt = tb * 4 + tt
                bank = pb[2 + tt % 2]
                if kt >= 2:
                    tb_p, tt_p = divmod(kt - 2, 4)
                    t.wait_ge(sem["pev"], tb_p * 8 + 4 + tt_p + 1)
                for j in range(DCH):
                    _li = t.matmul(bank[:, :256], ln_sb[:, j, tt * 128:(tt + 1) * 128],
                             wv_sb[:, j, :, :].rearrange("k h d -> k (h d)"),
                             start=(j == 0), stop=(j == DCH - 1))
                _li.then_inc(sem["pmm"], 1)
        nsc = 0
        for qb in range(NQB):
            for h in range(2):
                u = qb * 2 + h
                for kp in range(0, NKT, 2):
                    ip = nsc
                    nsc += 1
                    if ip >= 2:
                        t.wait_ge(sem["sev"], ip - 1)
                    if ip == 0:
                        t.wait_ge(sem["pev"], 8 * NTBa)
                    if h == 0 and kp == 0 and qb >= 1:
                        t.wait_ge(sem["wev"], DCH * qb)
                    bank = pb[ip % 2]
                    for z in range(2):
                        _li = t.matmul(bank[:, z * QB1:(z + 1) * QB1],
                                 kT[:, h, (kp + z) * 128:(kp + z + 1) * 128],
                                 qT[:, h, qb * QB1:(qb + 1) * QB1],
                                 start=(z == 0), stop=(z == 1))
                    _li.then_inc(sem["smm"], 1)
                t.wait_ge(sem["sev"], nsc)
                if u:
                    t.wait_ge(sem["attn"], u)
                for kp in range(0, NKT, 2):
                    _li = t.matmul(pb[4][:1, :512], ones128[:],
                             w1[:, kp:kp + 2, :].rearrange("k n q -> k (n q)"),
                             start=(kp == 0), stop=(kp + 2 >= NKT))
                _li.then_inc(sem["zmm"], 1)
                t.wait_ge(sem["zrec"], u + 1)
                _li = t.matmul(pb[5][:, :QB1], ones1[:], drow[:, 2, :QB1], start=True, stop=True)
                _li.then_inc(sem["zbc"], 1)
                for kt in range(NKT):
                    _li = t.matmul(pb[7][:, :QB1], v_sb[:, kt, h * 128:(h + 1) * 128],
                             w1[:, kt, :], start=(kt == 0), stop=(kt == NKT - 1))
                _li.then_inc(sem["avm"], 1)
            t.wait_ge(sem["attn"], 2 * (qb + 1))
            for dc in range(DCH):
                gi = qb * DCH + dc
                bank = pb[dc % 2]
                if gi >= 2:
                    t.wait_ge(sem["wev"], gi - 1)
                for hh in range(2):
                    _li = t.matmul(bank[:, :QB1], wo_sb[:, hh, dc * 128:(dc + 1) * 128],
                             att_sb[:, hh, :], start=(hh == 0), stop=(hh == 1))
                _li.then_inc(sem["womm"], 1)

    @block.scalar
    def _(sc):
        for tb in range(NTBa):
            for m in range(4):
                gi = tb * 4 + m
                sc.wait_ge(sem["pmm"], tb * 8 + m + 1)
                dst = (qT if m < 2 else kT)[:, m % 2, tb * TBa:(tb + 1) * TBa]
                _li = sc.copy(out=dst, in_=pb[m % 2][:, :])
                _li.then_inc(sem["pev"], 1)
            for tt in range(4):
                kt = tb * 4 + tt
                sc.wait_ge(sem["pmm"], tb * 8 + 4 + tt + 1)
                _li = sc.copy(out=v_sb[:, kt, :], in_=pb[2 + tt % 2][:, :256])
                _li.then_inc(sem["pev"], 1)
        nsc = 0
        for qb in range(NQB):
            for h in range(2):
                u = qb * 2 + h
                for kp in range(0, NKT, 2):
                    ip = nsc
                    nsc += 1
                    sc.wait_ge(sem["smm"], ip + 1)
                    _li = sc.activation(out=w1[:, kp:kp + 2, :].rearrange("k n q -> k (n q)"),
                                  in_=pb[ip % 2][:, :512], func=AF.Exp, scale=SC, bias=shift[:, :])
                    _li.then_inc(sem["sev"], 1)
                sc.wait_ge(sem["avm"], u + 1)
                _li = sc.copy(out=att_un[:], in_=pb[7][:, :QB1])
                _li.then_inc(sem["aev"], 1)
            for dc in range(DCH):
                gi = qb * DCH + dc
                sc.wait_ge(sem["womm"], gi + 1)
                if dc == 0 and qb >= 1:
                    sc.wait_ge(sem["pstr"], 16 * qb)
                _li = sc.copy(out=out_st[:, dc, :], in_=pb[dc % 2][:, :QB1])
                _li.then_inc(sem["wev"], 1)

    @block.vector
    def _(v):
        v.memset(ones128[:], 1.0)
        v.memset(ones1[:], 1.0)
        v.memset(shift[:], -4.0)
        for qb in range(NQB):
            for h in range(2):
                u = qb * 2 + h
                v.wait_ge(sem["zmm"], u + 1)
                v.tensor_copy(out=drow[:, 0, :], in_=pb[4][:1, 0:2 * QB1])
                v.tensor_tensor(out=drow[:, 0, :QB1], in0=drow[:, 0, :QB1],
                                in1=drow[:, 0, QB1:2 * QB1], op=ALU.add)
                v.reciprocal(out=drow[:, 2, :QB1], in_=drow[:, 0, :QB1]).then_inc(sem["zrec"], 1)
                v.wait_ge(sem["zbc"], u + 1)
                v.wait_ge(sem["aev"], u + 1)
                _li = v.tensor_tensor(out=att_sb[:, h, :], in0=att_un[:], in1=pb[5][:, :QB1], op=ALU.mult)
                _li.then_inc(sem["attn"], 1)

    ctx.close()
    return nc



def _prog(name, fn):
    if name not in _PROG:
        _PROG[name] = fn()
    return _PROG[name]

def _launch(nc, in_maps):
    return run_bass_kernel_spmd(nc, in_maps, core_ids=list(range(NC))).results

def _cmaj(a):
    return np.ascontiguousarray(a)

def kernel(**inputs):
    x0 = np.asarray(inputs["L0_attn_x"])[0]
    xT = _cmaj(x0.T.astype(np.float32, copy=False))
    x_sh = [_cmaj(xT[:, c * TS:(c + 1) * TS]) for c in range(NC)]

    wr_t = _cmaj(np.asarray(inputs["L0_ffn_Wr"]).T)
    rbias = _cmaj(np.asarray(inputs["L0_ffn_bias"]).reshape(1, N_EXP))
    W2 = np.asarray(inputs["L1_ffn_W2"])
    w2s_sh = [_cmaj(W2[c * 256:(c + 1) * 256, :].T) for c in range(NC)]
    Hm = np.array([[1.0]], np.float32)
    while Hm.shape[0] < 128:
        Hm = np.block([[Hm, Hm], [Hm, -Hm]])
    h128 = Hm.astype(BF)

    Wg = np.asarray(inputs["L0_ffn_Wg"]); Wu = np.asarray(inputs["L0_ffn_Wu"])
    Wd = np.asarray(inputs["L0_ffn_Wd"])
    sWg = np.asarray(inputs["L0_ffn_sWg"]); sWu = np.asarray(inputs["L0_ffn_sWu"])
    sWd = np.asarray(inputs["L0_ffn_sWd"])
    moe_w = []
    for c in range(NC):
        moe_w.append(dict(
            wg=_cmaj(Wg[2 * c:2 * c + 2].transpose(1, 0, 2)),
            wu=_cmaj(Wu[2 * c:2 * c + 2].transpose(1, 0, 2)),
            wd=_cmaj(Wd[2 * c:2 * c + 2]),
            swg=_cmaj(sWg[:, 128 * c:128 * (c + 1)]),
            swu=_cmaj(sWu[:, 128 * c:128 * (c + 1)]),
            swd=_cmaj(sWd[128 * c:128 * (c + 1), :]),
        ))

    Wq = np.asarray(inputs["L0_attn_Wq"]); Wk = np.asarray(inputs["L0_attn_Wk"])
    Wv = np.asarray(inputs["L0_attn_Wv"]); Wo = np.asarray(inputs["L0_attn_Wo"])
    lq1 = np.asarray(inputs["L0_attn_lq1"]); lq2 = np.asarray(inputs["L0_attn_lq2"])
    lk1 = np.asarray(inputs["L0_attn_lk1"]); lk2 = np.asarray(inputs["L0_attn_lk2"])
    rms_w = np.asarray(inputs["L0_attn_rms_w"])
    pidx = np.empty(256, np.int64)
    for a in range(2):
        for k in range(64):
            pidx[a * 128 + k] = 2 * (a * 64 + k)
            pidx[a * 128 + 64 + k] = 2 * (a * 64 + k) + 1
    theta = 1.0 / 10000.0 ** (np.arange(0, 256, 2, dtype=np.float64) / 256)
    ang = np.arange(T, dtype=np.float64)[:, None] * theta[None, :]
    cs = np.empty((64, 2, 2, T), np.float32)
    for a in range(2):
        cs[:, a, 0, :] = np.cos(ang[:, a * 64:(a + 1) * 64]).T
        cs[:, a, 1, :] = np.sin(ang[:, a * 64:(a + 1) * 64]).T
    kk = np.arange(128)[:, None, None]; jj = np.arange(2)[None, :, None]
    qq = np.arange(QB0)[None, None, :]
    allow = (jj * 128 + kk) <= qq
    maska = np.where(allow, np.float32(0), np.float32(-1e30)).astype(BF)
    a0_w = []
    for c in range(NC):
        hs = [2 * c, 2 * c + 1]
        g = c // 2
        wq_c = np.stack([Wq[:, h * 256:(h + 1) * 256][:, pidx] for h in hs], axis=1)
        a0_w.append(dict(
            wq=_cmaj(wq_c.reshape(D, 4, 128)),
            wk=_cmaj(Wk[:, g * 256:(g + 1) * 256][:, pidx].reshape(D, 2, 128)),
            wv=_cmaj(Wv[:, g * 128:(g + 1) * 128]),
            wo=_cmaj(Wo[2 * c * 128:(2 * c + 2) * 128, :]),
            lq1=_cmaj(lq1[hs].reshape(1, 2, 128)), lq2=_cmaj(lq2[hs].reshape(1, 2, 128)),
            lk1=_cmaj(lk1[hs].reshape(1, 2, 128)), lk2=_cmaj(lk2[hs].reshape(1, 2, 128)),
            rmsw=_cmaj(rms_w.reshape(128, 1)), cs=cs, maska=maska,
        ))

    Wqkv = np.asarray(inputs["L1_attn_Wqkv"]); Wout = np.asarray(inputs["L1_attn_Wout"])
    a1_w = []
    for c in range(NC):
        r0 = 2 * c * 128
        a1_w.append(dict(
            wqh=_cmaj(Wqkv[r0:r0 + 256, :].T.reshape(D, 2, 128)),
            wkh=_cmaj(Wqkv[2048 + r0:2048 + r0 + 256, :].T.reshape(D, 2, 128)),
            wvh=_cmaj(Wqkv[4096 + r0:4096 + r0 + 256, :].T.reshape(D, 2, 128)),
            woh=_cmaj(Wout[:, r0:r0 + 256].T),
        ))

    W1 = np.asarray(inputs["L1_ffn_W1"]); W3 = np.asarray(inputs["L1_ffn_W3"])
    f1_w = []
    for c in range(NC):
        f1_w.append(dict(
            w1s=_cmaj(W1[c * 1024:(c + 1) * 1024, :].T),
            w3s=_cmaj(W3[c * 1024:(c + 1) * 1024, :].T),
        ))

    def parts_for_entry(parts):
        P = np.stack(parts, axis=0).reshape(NC, DCH, 128, T)
        return [_cmaj(P[:, :, :, c * TS:(c + 1) * TS].transpose(1, 0, 2, 3)) for c in range(NC)]

    def ln_full(ln_shards):
        return _cmaj(np.concatenate(ln_shards, axis=1))

    def crow_pairs(c_shards):
        Cfull = np.concatenate(c_shards, axis=0)
        return [_cmaj(Cfull[:, [2 * c, 2 * c + 1]].T.reshape(1, 2, T)) for c in range(NC)]

    e0 = _launch(_prog("entry0", lambda: build_entry("entry0")),
                 [dict(x_in=x_sh[c], wr=wr_t, rbias=rbias, w2s=w2s_sh[c], h128=h128)
                  for c in range(NC)])
    ln0 = ln_full([r["ln_out"] for r in e0])
    crow0 = crow_pairs([r["c_out"] for r in e0])
    w2p_sh = [_cmaj(np.concatenate([np.asarray(e0[j]["w2p_o"])[c * 1024:(c + 1) * 1024, :]
                                    for j in range(NC)], axis=1)) for c in range(NC)]

    h1 = _launch(_prog("moe", build_moe),
                 [dict(ln=ln0, crow=crow0[c], **moe_w[c]) for c in range(NC)])

    e1 = _launch(_prog("entry_plain", lambda: build_entry("entry_plain")),
                 [dict(x_in=x_sh[c], parts=p) for c, p in
                  enumerate(parts_for_entry([r["part"] for r in h1]))])
    x_sh = [r["x_out"] for r in e1]
    ln1 = ln_full([r["ln_out"] for r in e1])

    h2 = _launch(_prog("attn0", build_attn0),
                 [dict(ln=ln1, **a0_w[c]) for c in range(NC)])

    e2 = _launch(_prog("entry_moe", lambda: build_entry("entry_moe")),
                 [dict(x_in=x_sh[c], parts=p, wr=wr_t, rbias=rbias) for c, p in
                  enumerate(parts_for_entry([r["part"] for r in h2]))])
    x_sh = [r["x_out"] for r in e2]
    ln2 = ln_full([r["ln_out"] for r in e2])
    crow2 = crow_pairs([r["c_out"] for r in e2])

    h3 = _launch(_PROG["moe"], [dict(ln=ln2, crow=crow2[c], **moe_w[c]) for c in range(NC)])

    e3 = _launch(_PROG["entry_plain"],
                 [dict(x_in=x_sh[c], parts=p) for c, p in
                  enumerate(parts_for_entry([r["part"] for r in h3]))])
    x_sh = [r["x_out"] for r in e3]
    ln3 = ln_full([r["ln_out"] for r in e3])

    h4 = _launch(_prog("ffn1", build_ffn1),
                 [dict(ln=ln3, w2p=w2p_sh[c], **f1_w[c]) for c in range(NC)])

    e4 = _launch(_PROG["entry_plain"],
                 [dict(x_in=x_sh[c], parts=p) for c, p in
                  enumerate(parts_for_entry([r["part"] for r in h4]))])
    x_sh = [r["x_out"] for r in e4]
    ln4 = ln_full([r["ln_out"] for r in e4])

    h5 = _launch(_prog("attn1", build_attn1),
                 [dict(ln=ln4, **a1_w[c]) for c in range(NC)])

    e5 = _launch(_PROG["entry_plain"],
                 [dict(x_in=x_sh[c], parts=p) for c, p in
                  enumerate(parts_for_entry([r["part"] for r in h5]))])
    x_sh = [r["x_out"] for r in e5]
    ln5 = ln_full([r["ln_out"] for r in e5])

    h6 = _launch(_PROG["ffn1"], [dict(ln=ln5, w2p=w2p_sh[c], **f1_w[c]) for c in range(NC)])

    e6 = _launch(_PROG["entry_plain"],
                 [dict(x_in=x_sh[c], parts=p) for c, p in
                  enumerate(parts_for_entry([r["part"] for r in h6]))])
    xT_out = np.concatenate([np.asarray(r["x_out"]) for r in e6], axis=1)
    return _cmaj(xT_out.T).reshape(1, T, D).astype(np.float32, copy=False)
